# version 03 (final)
# baseline (speedup 1.0000x reference)
import math
import numpy as np
import ml_dtypes
import concourse.bass as bass
import concourse.mybir as mybir
from concourse.bass_utils import run_bass_kernel_spmd

F32 = mybir.dt.float32
BF16 = mybir.dt.bfloat16
I8 = mybir.dt.int8
ALU = mybir.AluOpType
ACT = mybir.ActivationFunctionType
AXX = mybir.AxisListType.X

B, S, D, H, HD = 4, 2048, 2048, 16, 128
T = 1024
TC, SC, DC = 8, 16, 16
I0, I1 = 8192, 5504
NST, DTR = 8, 256
RSQ = 1.0 / math.sqrt(HD)
FNORM = 1.0 / math.sqrt(I0)
LNEPS = 1e-5
QEPS = 1e-5

ENGINES = ("sync", "vector", "scalar", "tensor", "gpsimd")
KB = 1024


NDSLOT = 8


class Prog:
    def __init__(self, nc):
        self.nc = nc
        self.ops = {e: [] for e in ENGINES}
        self.cnt = {e: 0 for e in ENGINES}
        self.dcnt = {e: [0] * NDSLOT for e in ENGINES}
        self.drr = {e: 0 for e in ENGINES}

    def emit(self, eng, fn, inc=1):
        self.cnt[eng] += inc
        self.ops[eng].append(("inc", fn, inc))
        return (eng, self.cnt[eng])

    def dma(self, dst, src, eng="sync"):
        sl = self.drr[eng]
        self.drr[eng] = (sl + 1) % NDSLOT
        self.dcnt[eng][sl] += 16
        self.ops[eng].append(("dma", lambda e, d=dst, s=src:
                              e.dma_start(out=d, in_=s), sl))
        return (eng + "_dma%d" % sl, self.dcnt[eng][sl])

    def wait(self, eng, token):
        if token is None:
            return
        if isinstance(token, list):
            for t in token:
                self.ops[eng].append(("wait", t[0], t[1]))
        else:
            self.ops[eng].append(("wait", token[0], token[1]))

    def last(self, eng):
        return (eng, self.cnt[eng])

    def selfwait(self, eng):
        self.ops[eng].append(("wait", eng, self.cnt[eng]))

    def lastd(self, eng):
        return [(eng + "_dma%d" % sl, self.dcnt[eng][sl])
                for sl in range(NDSLOT) if self.dcnt[eng][sl] > 0]

    def run(self):
        nc = self.nc
        cms, sems = [], {}
        try:
            for e in ENGINES:
                cm = nc.semaphore("sem_" + e)
                sems[e] = cm.__enter__(); cms.append(cm)
                for sl in range(NDSLOT):
                    cm = nc.semaphore("semd_%s%d" % (e, sl))
                    sems[e + "_dma%d" % sl] = cm.__enter__(); cms.append(cm)
            bcm = nc.Block(); block = bcm.__enter__(); cms.append(bcm)
            for e in ENGINES:
                ops, sem = self.ops[e], sems[e]

                def body(engine, ops=ops, sem=sem, e=e):
                    for op in ops:
                        k = op[0]
                        if k == "wait":
                            engine.wait_ge(sems[op[1]], op[2])
                        elif k == "dma":
                            op[1](engine).then_inc(
                                sems[e + "_dma%d" % op[2]], 16)
                        elif k == "inc":
                            op[1](engine).then_inc(sem, op[2])
                        else:
                            op[1](engine)

                getattr(block, e)(body)
        finally:
            for cm in reversed(cms):
                cm.__exit__(None, None, None)


N_STAGES = 4
P6_SUB = 9
DEBUG = False
LAST_RESULTS = None


class _StopEmit(Exception):
    pass


def build_nc():
    nc = bass.Bass("TRN2", target_bir_lowering=False)
    p = Prog(nc)

    def din(name, shape, dtype=F32):
        return nc.dram_tensor(name, shape, dtype, kind="ExternalInput").ap()

    x_own = din("x_own", [T, D])
    x_seq = din("x_seq", [S, D])
    wqT = din("wqT", [D, D]); wkT = din("wkT", [D, D])
    wvT = din("wvT", [D, D]); woT = din("woT", [D, D])
    bq_in = din("bq", [128, DC]); bk_in = din("bk", [128, DC])
    bv_in = din("bv", [128, DC]); bo_in = din("bo", [128, DC])
    slo_in = din("slo", [1, 16])
    master_in = din("master", [128, 2944])
    wgT_in = din("wgT", [D, I0]); wuT_in = din("wuT", [D, I0])
    wdT_in = din("wdT", [I0, D])
    wxpT = din("wxpT", [D, 272]); wdtT = din("wdtT", [DTR, D])
    bdt_in = din("bdt", [128, DC])
    alog_in = din("alog", [128, DC * NST])
    dpar_in = din("dpar", [128, DC])
    flag_in = din("flag", [128, 1])
    wg1T_in = din("wg1T", [D, I1]); wu1T_in = din("wu1T", [D, I1])
    wd1T_in = din("wd1T", [I1, D])
    h128_in = din("h128c", [128, 128], BF16)
    i2h64_in = din("i2h64c", [128, 128], BF16)
    ident_in = din("identc", [128, 128], BF16)
    onec_in = din("onecc", [128, 1], BF16)
    oner_in = din("onerc", [1, 128])
    identf_in = din("identfc", [128, 128])
    selc_in = din("selc", [16, 2048], BF16)
    out = nc.dram_tensor("out", [T, D], F32, kind="ExternalOutput").ap()

    ctx_scr = nc.dram_tensor("ctx_scr", [2, 128, 8 * T], BF16,
                             kind="ExternalOutput" if DEBUG else "Internal").ap()
    if DEBUG:
        dbg_q = nc.dram_tensor("dbg_q", [2, 128, 8 * T], BF16,
                               kind="ExternalOutput").ap()
        dbg_xh = nc.dram_tensor("dbg_xh", [128, DC * S], BF16,
                                kind="ExternalOutput").ap()
        dbg_st = nc.dram_tensor("dbg_st", [128, 4 * SC], F32,
                                kind="ExternalOutput").ap()
        dbg_xq = nc.dram_tensor("dbg_xq", [128, DC * T], BF16,
                                kind="ExternalOutput").ap()
        dbg_kk = nc.dram_tensor("dbg_kk", [128, 8 * S], BF16,
                                kind="ExternalOutput").ap()
        dbg_dn = nc.dram_tensor("dbg_dn", [128, 3 * T], F32,
                                kind="ExternalOutput").ap()
        dbg_vv = nc.dram_tensor("dbg_vv", [128, 8 * S], BF16,
                                kind="ExternalOutput").ap()
        dbg_i = nc.dram_tensor("dbg_i", [2, 128, 64 * 256], BF16,
                               kind="ExternalOutput").ap()
        dbg_z = nc.dram_tensor("dbg_z", [128, 4096], BF16,
                               kind="ExternalOutput").ap()
        dbg_t = nc.dram_tensor("dbg_t", [128, 2048], BF16,
                               kind="ExternalOutput").ap()
        dbg_x2 = nc.dram_tensor("dbg_x2", [128, DC * T], BF16,
                                kind="ExternalOutput").ap()
        dbg_tw = nc.dram_tensor("dbg_tw", [128, 16384], BF16,
                                kind="ExternalOutput").ap()
        dbg_w = nc.dram_tensor("dbg_w", [128, T + 16 + TC], F32,
                               kind="ExternalOutput").ap()
    cc_in = nc.dram_tensor("cc_in", [128, 256], F32, kind="Internal").ap()
    cc_out = nc.dram_tensor("cc_out", [128, 256], F32, kind="Internal").ap()
    tern_scr = {}
    for nm, shp in (("g", [D, I0]), ("u", [D, I0]), ("d", [I0, D]),
                    ("g1", [D, I1]), ("u1", [D, I1]), ("d1", [I1, D])):
        tern_scr[nm] = nc.dram_tensor("tern_" + nm, shp, BF16, kind="Internal").ap()

    SBASE = 17 * KB

    def sba(name, shape, dtype, off):
        return nc.alloc_sbuf_tensor_at(name, shape, dtype, offset=off + SBASE)

    o = [SBASE]

    def sb(name, shape, dtype):
        t = nc.alloc_sbuf_tensor_at(name, shape, dtype, offset=o[0])
        nb = 1
        for s_ in shape[1:]:
            nb *= s_
        nb *= mybir.dt.size(dtype)
        o[0] += (nb + 31) // 32 * 32
        return t

    bqs = sb("bqs", [128, DC], F32); bks = sb("bks", [128, DC], F32)
    bvs = sb("bvs", [128, DC], F32); bos = sb("bos", [128, DC], F32)
    bdts = sb("bdts", [128, DC], F32)
    aneg = sb("aneg", [128, DC * NST], F32)
    dpar = sb("dpar", [128, DC], F32)
    flag = sb("flag", [128, 1], F32)
    slo_row = sb("slo_row", [1, 16], F32)
    slo_rep = sb("slo_rep", [128, 16], F32)
    h128 = sb("h128", [128, 128], BF16)
    i2h64 = sb("i2h64", [128, 128], BF16)
    ident = sb("ident", [128, 128], BF16)
    identf = sb("identf", [128, 128], F32)
    onec = sb("onec", [128, 1], BF16)
    oner = sb("oner", [1, 128], F32)
    fngm = sb("fngm", [128, SC], F32); frst = sb("frst", [128, SC], F32)
    ongm = sb("ongm", [128, TC], F32); orst = sb("orst", [128, TC], F32)
    ocol = sb("ocol", [128, SC], F32); ocol2 = sb("ocol2", [128, SC], F32)
    scol = sb("scol", [128, SC], F32)
    wsc = sb("wsc", [128, 16], F32)
    rowb = sb("rowb", [1, 512], F32)
    rowc = sb("rowc", [1, 512], F32)
    invs = sb("invs", [128, T], F32)
    invs2 = sb("invs2", [128, T], F32)
    hstate = sb("hstate", [128, 128], F32)
    cdcar = sb("cdcar", [128, DC], F32)
    fsend = sb("fsend", [128, 256], F32)
    frecv = sb("frecv", [128, 256], F32)
    fw = sb("fw", [128, 128], F32)
    zero64 = sb("zero64", [128, 256], F32)
    half = sb("half", [128, 512], F32)
    sc_a = sb("sc_a", [128, 256], F32)
    sc_b = sb("sc_b", [128, 256], F32)
    sc_c = sb("sc_c", [128, 256], F32)
    sc_d = sb("sc_d", [128, 256], BF16)
    sc_e = sb("sc_e", [128, 256], F32)
    sc_i8 = sb("sc_i8", [128, 256], I8)
    ttile = sb("ttile", [128, 128], BF16)
    ttile2 = sb("ttile2", [128, 128], BF16)
    cpool_vals = [LNEPS, RSQ, 1.0 / 30.0, 30.0, -10.0, -1.0, 1.0, 0.0, 0.5,
                  1.0 / 128.0, FNORM, 2.0, -2.0]
    cpool = {}
    for i_, v_ in enumerate(cpool_vals):
        cpool[v_] = sb("cp%d" % i_, [128, 1], F32)
        nc.const_aps.register(F32, v_, cpool[v_][:, 0:1])
    PERM_END = o[0] - SBASE
    assert PERM_END <= 28 * KB, PERM_END

    p_sb = sba("p_sb", [128, 2048], BF16, 28 * KB)
    u_sb = sba("u_sb", [128, 1024], F32, 32 * KB)
    st_a = sba("st_a", [128, 1024], F32, 36 * KB)
    st_b = sba("st_b", [128, 1024], F32, 40 * KB)
    st_c = sba("st_c", [128, 1024], F32, 44 * KB)
    st_d = sba("st_d", [128, 1024], F32, 48 * KB)
    st_wb = sba("st_wb", [128, 2048], BF16, 52 * KB)
    st_ef = sba("st_ef", [128, 2048], BF16, 28 * KB)
    z2s = sba("z2s", [128, 4096], BF16, 28 * KB)
    st_g = sba("st_g", [128, 2048], BF16, 36 * KB)
    st_h = sba("st_h", [128, 2048], BF16, 40 * KB)
    z1s = sba("z1s", [128, 4096], BF16, 44 * KB)
    i8s = sba("i8s", [128, 4096], I8, 52 * KB)
    i8b = sba("i8b", [128, 2048], I8, 44 * KB)
    expb = sba("expb", [128, NST * 256], F32, 28 * KB)
    b_rep = sba("b_rep", [128, NST * 256], F32, 36 * KB)
    hbuf = sba("hbuf", [128, NST * 256], F32, 44 * KB)
    wdtb = sba("wdtb", [128, 4096], BF16, 44 * KB)
    st_x = sba("st_x", [128, 1024], F32, 52 * KB)
    XR = 56 * KB
    x_res = sba("x_res", [128, TC * D], F32, XR)
    xhat_fm = sba("xhat_fm", [128, DC * S], BF16, XR)
    xq_own = sba("xq_own", [128, DC * T], BF16, XR)
    ctxh = sba("ctxh", [128, 8 * T], BF16, XR)
    qbuf = sba("qbuf", [128, 8 * T], BF16, XR + 32 * KB)
    den_row = sba("den_row", [1, T], F32, XR + 48 * KB)
    rec_row = sba("rec_row", [1, T], F32, XR + 52 * KB)
    AR3 = 120 * KB
    kbuf = sba("kbuf", [128, 8 * S], BF16, AR3)
    ctx = sba("ctx", [128, DC * T], BF16, AR3)
    xq_fm = sba("xq_fm", [128, DC * T], BF16, AR3)
    xln3 = sba("xln3", [128, DC * T], BF16, AR3)
    xq1_fm = sba("xq1_fm", [128, DC * T], BF16, AR3)
    AR4 = 152 * KB
    vbuf = sba("vbuf", [128, 8 * S], BF16, AR4)
    inter = sba("inter", [128, 64 * 256], BF16, AR4)
    delta = sba("delta", [128, DC * 256], F32, AR4)
    du = sba("du", [128, DC * 256], F32, AR4 + 16 * KB)
    hid = sba("hid", [128, 43 * 256], BF16, AR4)
    xsc5 = sba("xsc5", [128, 2048], F32, AR4)
    MR = 184 * KB
    master = sba("master", [128, 2944], F32, MR)
    selc = sba("selc", [16, 2048], BF16, 197 * KB)
    c_rep = sba("c_rep", [128, NST * 256], F32, MR)

    ps = [nc.alloc_psum_tensor("ps%d" % i, [128, 512], F32) for i in range(6)]
    ps_bf = nc.alloc_psum_tensor("ps_bf", [128, 512], BF16)
    ps_bf2 = nc.alloc_psum_tensor("ps_bf2", [128, 512], BF16)

    V, Sg, Tg, G = "vector", "scalar", "tensor", "gpsimd"

    def ts(eng, out_ap, in_ap, s1, s2, op0, op1=None, inc=1):
        if op1 is None:
            return p.emit(eng, lambda e: e.tensor_scalar(
                out=out_ap, in0=in_ap, scalar1=s1, scalar2=None, op0=op0), inc)
        return p.emit(eng, lambda e: e.tensor_scalar(
            out=out_ap, in0=in_ap, scalar1=s1, scalar2=s2, op0=op0, op1=op1), inc)

    def stt(eng, out_ap, in0_ap, scalar, in1_ap, op0, op1, inc=1):
        return p.emit(eng, lambda e: e.scalar_tensor_tensor(
            out=out_ap, in0=in0_ap, scalar=scalar, in1=in1_ap, op0=op0, op1=op1),
            inc)

    def tt(eng, out_ap, a_ap, b_ap, op, inc=1):
        return p.emit(eng, lambda e: e.tensor_tensor(
            out=out_ap, in0=a_ap, in1=b_ap, op=op), inc)

    def act(out_ap, in_ap, func, bias=0.0, scale=1.0, accum=None, inc=1):
        return p.emit(Sg, lambda e: e.activation(
            out=out_ap, in_=in_ap, func=func, bias=bias, scale=scale,
            accum_out=accum), inc)

    def cp(eng, out_ap, in_ap, inc=1):
        return p.emit(eng, lambda e: e.tensor_copy(out=out_ap, in_=in_ap), inc)

    def mm(out_ap, lhsT_ap, rhs_ap, start, stop, inc=1):
        return p.emit(Tg, lambda e: e.matmul(
            out_ap, lhsT_ap, rhs_ap, start=start, stop=stop), inc)

    def tr(out_ap, in_ap, inc=1):
        return p.emit(Tg, lambda e: e.transpose(out_ap, in_ap, ident[:, :]), inc)

    def vrecip(out_ap, in_ap, inc=1):
        return p.emit(V, lambda e: e.reciprocal(out=out_ap, in_=in_ap), inc)

    def red(eng, out_ap, in_ap, op, absv=False, inc=1):
        return p.emit(eng, lambda e: e.tensor_reduce(
            out=out_ap, in_=in_ap, axis=AXX, op=op,
            apply_absolute_value=absv), inc)

    def repl_row(row_ap, dst_ap, n, pb=5):
        tok = p.last(V)
        p.wait(Tg, tok)
        t = mm(ps[pb][:, 0:n], oner[:, :], row_ap, True, True)
        p.wait(V, t)
        return cp(V, dst_ap, ps[pb][:, 0:n])

    def col_to_row(col_ap, pb=5):
        t = p.emit(Tg, lambda e: e.transpose(
            ps[pb][0:1, 0:128].rearrange("a b -> a b"), col_ap, ident[:, :]))
        p.wait(V, t)
        return cp(V, rowb[0:1, 0:128], ps[pb][0:1, 0:128])

    for v_, t_ in cpool.items():
        p.emit(V, lambda e, t_=t_, v_=v_: e.memset(t_[:, :], v_))
    p.emit(V, lambda e: e.memset(zero64[:, :], 0.0))
    p.emit(V, lambda e: e.memset(hstate[:, :], 0.0))
    tok_cp = p.emit(V, lambda e: e.memset(cdcar[:, :], 0.0))
    p.wait(Sg, tok_cp); p.wait(Tg, tok_cp); p.wait(G, tok_cp)
    p.dma(bqs[:, :], bq_in); p.dma(bks[:, :], bk_in)
    p.dma(bvs[:, :], bv_in); p.dma(bos[:, :], bo_in)
    p.dma(bdts[:, :], bdt_in)
    p.dma(aneg[:, :], alog_in)
    p.dma(dpar[:, :], dpar_in)
    p.dma(flag[:, :], flag_in)
    p.dma(slo_row[:, :], slo_in)
    p.dma(master[:, :], master_in)
    p.dma(h128[:, :], h128_in); p.dma(i2h64[:, :], i2h64_in)
    p.dma(ident[:, :], ident_in); p.dma(onec[:, :], onec_in)
    p.dma(identf[:, :], identf_in)
    p.dma(selc[:, :], selc_in)
    tok_l = p.dma(oner[:, :], oner_in)
    p.wait(V, tok_l); p.wait(Sg, tok_l); p.wait(Tg, tok_l)
    ts(V, bqs[:, :], bqs[:, :], RSQ, None, ALU.mult)
    tok_ae = act(aneg[:, :], aneg[:, :], ACT.Exp)
    p.wait(V, tok_ae)
    tok = ts(V, aneg[:, :], aneg[:, :], -1.0, None, ALU.mult)
    p.wait(V, tok_l)
    cp(V, rowb[0:1, 0:16], slo_row[0:1, 0:16])
    p.selfwait(V)
    ts(V, rowb[0:1, 0:16], rowb[0:1, 0:16], RSQ, None, ALU.mult)
    repl_row(rowb[0:1, 0:16], slo_rep[:, :], 16)

    def tm_stats(src_dram, nchunks, negm_t, rstd_t):
        prevs = {}
        for tc_ in range(nchunks):
            sl = st_a if tc_ % 2 == 0 else st_b
            sl2 = st_c if tc_ % 2 == 0 else st_d
            if tc_ >= 2:
                p.wait("sync", prevs[tc_ - 2][0])
                p.wait("sync", prevs[tc_ - 2][1])
            tok = p.dma(sl[:, :], src_dram[tc_ * 128:(tc_ + 1) * 128, 0:1024])
            tok2 = p.dma(sl2[:, :], src_dram[tc_ * 128:(tc_ + 1) * 128, 1024:2048])
            if tc_ >= 1:
                p.wait(V, prevs[tc_ - 1][1])
            p.wait(Sg, tok); p.wait(Sg, tok2)
            act(u_sb[:, :], sl[:, :], ACT.Square, accum=ocol[:, tc_:tc_ + 1])
            act(u_sb[:, :], sl2[:, :], ACT.Square, accum=ocol2[:, tc_:tc_ + 1])
            p.wait(V, tok); p.wait(V, tok2)
            red(V, scol[:, 0:1], sl[:, :], ALU.add)
            red(V, scol[:, 1:2], sl2[:, :], ALU.add)
            p.selfwait(V)
            tt(V, scol[:, 0:1], scol[:, 0:1], scol[:, 1:2], ALU.add)
            tok_s = p.last(Sg)
            p.wait(V, tok_s)
            tt(V, ocol[:, tc_:tc_ + 1], ocol[:, tc_:tc_ + 1],
               ocol2[:, tc_:tc_ + 1], ALU.add)
            p.selfwait(V)
            ts(V, negm_t[:, tc_:tc_ + 1], scol[:, 0:1], -1.0 / D, None, ALU.mult)
            p.selfwait(V)
            tt(V, scol[:, 2:3], negm_t[:, tc_:tc_ + 1], negm_t[:, tc_:tc_ + 1],
               ALU.mult)
            p.selfwait(V)
            ts(V, ocol[:, tc_:tc_ + 1], ocol[:, tc_:tc_ + 1], 1.0 / D, None,
               ALU.mult)
            p.selfwait(V)
            tok_v = tt(V, scol[:, 2:3], ocol[:, tc_:tc_ + 1], scol[:, 2:3],
                       ALU.subtract)
            p.wait(Sg, tok_v)
            tok_s2 = act(scol[:, 3:4], scol[:, 2:3], ACT.Sqrt, bias=LNEPS)
            p.wait(V, tok_s2)
            tok_v2 = vrecip(rstd_t[:, tc_:tc_ + 1], scol[:, 3:4])
            prevs[tc_] = (tok_v2, tok_s2)

    tm_stats(x_seq, SC, fngm, frst)
    tm_stats(x_own, TC, ongm, orst)
    if DEBUG:
        tok_dv = p.last(V)
        p.wait("sync", tok_dv)
        p.dma(dbg_st[:, 0:SC], fngm[:, :])
        p.dma(dbg_st[:, SC:2 * SC], frst[:, :])
        p.dma(dbg_st[:, 2 * SC:2 * SC + TC], ongm[:, :])
        p.dma(dbg_st[:, 3 * SC:3 * SC + TC], orst[:, :])

    def build_xhat_fm(src_dram, ntc, negm_t, rstd_t, dst_fm, width):
        p.wait("sync", p.last(V))
        prev_v = {}
        for tc_ in range(ntc):
            sl = st_a if tc_ % 2 == 0 else st_b
            sl2 = st_c if tc_ % 2 == 0 else st_d
            if tc_ >= 2:
                p.wait("sync", prev_v[tc_ - 2])
            tok = p.dma(sl[:, :], src_dram[tc_ * 128:(tc_ + 1) * 128, 0:1024])
            tok2 = p.dma(sl2[:, :], src_dram[tc_ * 128:(tc_ + 1) * 128, 1024:2048])
            p.wait(V, tok); p.wait(V, tok2)
            p.wait(V, p.last(Tg))
            ts(V, st_wb[:, 0:1024], sl[:, :], negm_t[:, tc_:tc_ + 1],
               rstd_t[:, tc_:tc_ + 1], ALU.add, ALU.mult)
            tok_v = ts(V, st_wb[:, 1024:2048], sl2[:, :], negm_t[:, tc_:tc_ + 1],
                       rstd_t[:, tc_:tc_ + 1], ALU.add, ALU.mult)
            prev_v[tc_] = tok_v
            p.wait(Tg, tok_v)
            vts = {}
            for dc_ in range(DC):
                pb = ps_bf if dc_ % 2 == 0 else ps_bf2
                if dc_ >= 2:
                    p.wait(Tg, vts[dc_ - 2])
                t_t = tr(pb[:, 0:128], st_wb[:, dc_ * 128:(dc_ + 1) * 128])
                p.wait(V, t_t)
                vts[dc_] = cp(V, dst_fm[:, dc_ * width + tc_ * 128:
                                        dc_ * width + tc_ * 128 + 128],
                              pb[:, 0:128])

    def proj_fm(w_dram, colblk, bias_t, bias_col, scale, dst, dst_off, src_fm,
                src_w, nt512, pb0=0, out_bf=True):
        c0 = colblk * 128
        p.wait("sync", p.last(V))
        wv3 = w_dram[:, c0:c0 + 128].rearrange("(a p) b -> p a b", p=128)
        tok = p.dma(st_a[:, :], wv3[:, 0:8, :])
        tok2 = p.dma(st_b[:, :], wv3[:, 8:16, :])
        p.wait(V, tok); p.wait(V, tok2)
        p.wait(V, p.last(Tg))
        cp(V, st_wb[:, 0:1024], st_a[:, :])
        tok_v = cp(V, st_wb[:, 1024:2048], st_b[:, :])
        p.wait(Tg, tok_v)
        p.wait(Tg, p.last(Sg))
        for dc_ in range(DC):
            for t5 in range(nt512):
                mm(ps[pb0 + t5][:, :], st_wb[:, dc_ * 128:(dc_ + 1) * 128],
                   src_fm[:, dc_ * src_w + t5 * 512: dc_ * src_w + t5 * 512 + 512],
                   start=(dc_ == 0), stop=(dc_ == DC - 1))
        tok_t = p.last(Tg)
        p.wait(Sg, tok_t)
        last = None
        for t5 in range(nt512):
            last = act(dst[:, dst_off + t5 * 512: dst_off + t5 * 512 + 512],
                       ps[pb0 + t5][:, :], ACT.Identity,
                       bias=bias_t[:, bias_col:bias_col + 1], scale=scale)
        return last

    for hh in range(2):
        p.wait(V, p.lastd("sync"))
        p.wait(Tg, p.lastd("sync"))
        build_xhat_fm(x_seq, SC, fngm, frst, xhat_fm, S)
        tok_xh = p.last(V)
        if DEBUG and hh == 0:
            p.wait("sync", tok_xh)
            p.dma(dbg_xh[:, :], xhat_fm[:, :])
        p.wait(Tg, tok_xh)
        for ocl in range(8):
            oc = hh * 8 + ocl
            proj_fm(wkT, oc, bks, oc, 1.0, kbuf, ocl * S, xhat_fm, S, 4)
            tok_v = proj_fm(wvT, oc, bvs, oc, 1.0, st_ef, 0, xhat_fm, S, 4)
            p.wait(Tg, tok_v)
            vts = {}
            for tc_ in range(SC):
                pb = ps_bf if tc_ % 2 == 0 else ps_bf2
                if tc_ >= 2:
                    p.wait(Tg, vts[tc_ - 2])
                t_t = tr(pb[:, 0:128], st_ef[:, tc_ * 128:(tc_ + 1) * 128])
                p.wait(V, t_t)
                vts[tc_] = cp(V, vbuf[:, tc_ * 1024 + ocl * 128:
                                      tc_ * 1024 + ocl * 128 + 128], pb[:, 0:128])
            p.wait(Tg, vts[SC - 1]); p.wait(Tg, vts[SC - 2])
        if DEBUG and hh == 0:
            tok_pv = p.last(V)
            p.wait("sync", tok_pv)
            p.dma(dbg_vv[:, :], vbuf[:, :])
        build_xhat_fm(x_own, TC, ongm, orst, xq_own, T)
        tok_xq = p.last(V)
        if DEBUG and hh == 0:
            p.wait("sync", tok_xq)
            p.dma(dbg_xq[:, :], xq_own[:, :])
            p.wait("sync", p.lastd("sync"))
        p.wait(Tg, tok_xq)
        for ocl in range(8):
            oc = hh * 8 + ocl
            proj_fm(wqT, oc, bqs, oc, RSQ, qbuf, ocl * T, xq_own, T, 2)

        for ocl in range(8):
            h_g = hh * 8 + ocl
            p.wait(Tg, p.last(V))
            p.wait(Tg, p.last(Sg))
            for tkc in range(SC):
                for t5 in range(2):
                    mm(ps[t5][:, :],
                       kbuf[:, ocl * S + tkc * 128: ocl * S + tkc * 128 + 128],
                       qbuf[:, ocl * T + t5 * 512: ocl * T + t5 * 512 + 512],
                       start=True, stop=True)
                tok_t = p.last(Tg)
                p.wait(V, tok_t)
                for t5 in range(2):
                    w0 = (t5 * 512) - tkc * 128 + 1920
                    stt(V, u_sb[:, t5 * 512:t5 * 512 + 512],
                        master[:, w0:w0 + 512], slo_rep[:, h_g:h_g + 1],
                        ps[t5][:, :], ALU.mult, ALU.add)
                tok_v = p.last(V)
                p.wait(Sg, tok_v)
                pslot = (tkc % 2) * 1024
                for t5 in range(2):
                    act(half[:, :], u_sb[:, t5 * 512:t5 * 512 + 512], ACT.Tanh,
                        scale=1.0 / 30.0)
                    tok_s = act(p_sb[:, pslot + t5 * 512:pslot + t5 * 512 + 512],
                                half[:, :], ACT.Exp, scale=30.0, bias=-10.0)
                p.wait(Tg, tok_s)
                for t5 in range(2):
                    mm(ps[2 + t5][:, :],
                       vbuf[:, tkc * 1024 + ocl * 128: tkc * 1024 + ocl * 128 + 128],
                       p_sb[:, pslot + t5 * 512: pslot + t5 * 512 + 512],
                       start=(tkc == 0), stop=(tkc == SC - 1))
                    mm(ps[4 + t5][0:1, :], onec[:, :],
                       p_sb[:, pslot + t5 * 512: pslot + t5 * 512 + 512],
                       start=(tkc == 0), stop=(tkc == SC - 1))
            tok_t = p.last(Tg)
            p.wait(V, tok_t)
            cp(V, ctxh[:, ocl * T:ocl * T + 512], ps[2][:, :])
            cp(V, ctxh[:, ocl * T + 512:ocl * T + 1024], ps[3][:, :])
            cp(V, den_row[0:1, 0:512], ps[4][0:1, :])
            cp(V, den_row[0:1, 512:1024], ps[5][0:1, :])
            p.selfwait(V)
            tok_r = vrecip(rec_row[0:1, :], den_row[0:1, :])
            p.wait(Tg, tok_r)
            t_m = None
            for t5 in range(2):
                t_m = mm(ps[t5][:, :], oner[:, :],
                         rec_row[0:1, t5 * 512:t5 * 512 + 512], True, True)
            p.wait(V, t_m)
            cp(V, invs[:, 0:512], ps[0][:, :])
            cp(V, invs[:, 512:1024], ps[1][:, :])
            stt(V, ctxh[:, ocl * T:ocl * T + 1024],
                ctxh[:, ocl * T:ocl * T + 1024],
                1.0, invs[:, :], ALU.bypass, ALU.mult)
        tok_v = p.last(V)
        p.wait("sync", tok_v)
        p.dma(ctx_scr[hh, :, :], ctxh[:, :])
        if DEBUG:
            p.dma(dbg_q[hh, :, :], qbuf[:, :])
            if hh == 0:
                p.dma(dbg_kk[:, :], kbuf[:, :])
                p.dma(dbg_dn[0:1, 0:T], den_row[0:1, :])
                p.dma(dbg_dn[0:1, T:2 * T], rec_row[0:1, :])
                p.dma(dbg_dn[:, 2 * T:3 * T], invs[:, :])


    tok_cd = p.lastd("sync")
    p.wait("sync", tok_cd)
    for tc_ in range(TC):
        p.dma(x_res[:, tc_ * D:(tc_ + 1) * D],
              x_own[tc_ * 128:(tc_ + 1) * 128, :])
    for hh in range(2):
        p.dma(ctx[:, hh * 8 * T:(hh + 1) * 8 * T], ctx_scr[hh, :, :])
    tok_ld = p.lastd("sync")

    p.wait(Tg, tok_ld)
    p.wait(V, tok_ld)
    for oc in range(DC):
        c0 = oc * 128
        p.wait("sync", p.last(V))
        wv3 = woT[:, c0:c0 + 128].rearrange("(a p) b -> p a b", p=128)
        tok = p.dma(st_a[:, :], wv3[:, 0:8, :])
        tok2 = p.dma(st_b[:, :], wv3[:, 8:16, :])
        p.wait(V, tok); p.wait(V, tok2)
        p.wait(V, p.last(Tg))
        cp(V, st_wb[:, 0:1024], st_a[:, :])
        tok_v = cp(V, st_wb[:, 1024:2048], st_b[:, :])
        p.wait(Tg, tok_v)
        p.wait(Tg, p.last(Sg))
        for dc_ in range(DC):
            for t5 in range(2):
                mm(ps[t5][:, :], st_wb[:, dc_ * 128:(dc_ + 1) * 128],
                   ctx[:, dc_ * T + t5 * 512: dc_ * T + t5 * 512 + 512],
                   start=(dc_ == 0), stop=(dc_ == DC - 1))
        tok_t = p.last(Tg)
        p.wait(Sg, tok_t)
        tok_s = None
        for t5 in range(2):
            tok_s = act(st_ef[:, t5 * 512:t5 * 512 + 512], ps[t5][:, :],
                        ACT.Identity, bias=bos[:, oc:oc + 1])
        p.wait(Tg, tok_s)
        vts = {}
        for tc_ in range(TC):
            pb = ps_bf if tc_ % 2 == 0 else ps_bf2
            if tc_ >= 2:
                p.wait(Tg, vts[tc_ - 2])
            t_t = tr(pb[:, 0:128], st_ef[:, tc_ * 128:(tc_ + 1) * 128])
            p.wait(V, t_t)
            sl = x_res[:, tc_ * D + oc * 128: tc_ * D + oc * 128 + 128]
            vts[tc_] = stt(V, sl, pb[:, 0:128], 1.0, sl, ALU.bypass, ALU.add)
        p.wait(Tg, vts[TC - 1]); p.wait(Tg, vts[TC - 2])


    def w0_quant(w_dram, nrow_chunks, ncol, dst_dram, slot):
        p.emit(V, lambda e: e.memset(scol[:, 0:1], 0.0))
        nseg = ncol // 2048
        for rc_ in range(nrow_chunks):
            for sg in range(nseg):
                c0 = sg * 2048
                p.wait("sync", p.last(Sg))
                p.wait("sync", p.last(V))
                t1 = p.dma(st_a[:, :], w_dram[rc_ * 128:(rc_ + 1) * 128,
                                              c0:c0 + 1024])
                t2 = p.dma(st_b[:, :], w_dram[rc_ * 128:(rc_ + 1) * 128,
                                              c0 + 1024:c0 + 2048])
                p.wait(Sg, t1); p.wait(Sg, t2)
                act(st_ef[:, 0:1024], st_a[:, :], ACT.Abs,
                    accum=scol[:, 1:2])
                tok_s = act(st_ef[:, 1024:2048], st_b[:, :], ACT.Abs,
                            accum=scol[:, 2:3])
                p.wait(V, tok_s)
                p.selfwait(V)
                tt(V, scol[:, 1:2], scol[:, 1:2], scol[:, 2:3], ALU.add)
                p.selfwait(V)
                tt(V, scol[:, 0:1], scol[:, 0:1], scol[:, 1:2], ALU.add)
        p.selfwait(V)
        tok_v = cp(V, ttile[:, 0:1], scol[:, 0:1])
        p.wait(Tg, tok_v)
        tok_t = mm(ps[5][0:1, 0:1], onec[:, :], ttile[:, 0:1], True, True)
        p.wait(V, tok_t)
        cp(V, rowb[0:1, 0:1], ps[5][0:1, 0:1])
        p.selfwait(V)
        ts(V, rowb[0:1, 0:1], rowb[0:1, 0:1],
           1.0 / (nrow_chunks * 128 * ncol), QEPS, ALU.mult, ALU.max)
        repl_row(rowb[0:1, 0:1], wsc[:, slot + 3:slot + 4], 1)
        p.selfwait(V)
        ts(V, wsc[:, slot:slot + 1], wsc[:, slot + 3:slot + 4], -1.0, None,
           ALU.mult)
        ts(V, wsc[:, slot + 3:slot + 4], wsc[:, slot + 3:slot + 4], 0.5, None,
           ALU.mult)
        p.selfwait(V)
        tok_m = ts(V, wsc[:, slot + 6:slot + 7], wsc[:, slot + 3:slot + 4],
                   -1.0, None, ALU.mult)
        for rc_ in range(nrow_chunks):
            for sg in range(nseg):
                c0 = sg * 2048
                p.wait("sync", p.last(V))
                t1 = p.dma(st_a[:, :], w_dram[rc_ * 128:(rc_ + 1) * 128,
                                              c0:c0 + 1024])
                t2 = p.dma(st_b[:, :], w_dram[rc_ * 128:(rc_ + 1) * 128,
                                              c0 + 1024:c0 + 2048])
                p.wait(V, t1); p.wait(V, t2)
                p.wait(V, tok_m)
                p.wait(V, p.lastd("gpsimd"))
                ts(V, st_ef[:, 0:1024], st_a[:, :], wsc[:, slot + 3:slot + 4],
                   None, ALU.is_ge)
                ts(V, st_ef[:, 1024:2048], st_b[:, :],
                   wsc[:, slot + 3:slot + 4], None, ALU.is_ge)
                stt(V, st_wb[:, 0:1024], st_a[:, :],
                    wsc[:, slot + 6:slot + 7], st_ef[:, 0:1024],
                    ALU.is_le, ALU.subtract)
                tok_v = stt(V, st_wb[:, 1024:2048], st_b[:, :],
                            wsc[:, slot + 6:slot + 7], st_ef[:, 1024:2048],
                            ALU.is_le, ALU.subtract)
                p.wait("gpsimd", tok_v)
                p.dma(dst_dram[rc_ * 128:(rc_ + 1) * 128, c0:c0 + 2048],
                      st_wb[:, :], eng="gpsimd")

    def w1_quant(w_dram, nrow_chunks, ncol, dst_dram):
        nseg = (ncol + 2047) // 2048
        for rc_ in range(nrow_chunks):
            for sg in range(nseg):
                c0 = sg * 2048
                w_ = min(2048, ncol - c0)
                hw = w_ // 2
                p.wait("sync", p.last(V))
                t1 = p.dma(st_a[:, 0:hw], w_dram[rc_ * 128:(rc_ + 1) * 128,
                                                 c0:c0 + hw])
                t2 = p.dma(st_b[:, 0:hw], w_dram[rc_ * 128:(rc_ + 1) * 128,
                                                 c0 + hw:c0 + w_])
                p.wait(Sg, t1); p.wait(Sg, t2)
                p.wait(Sg, p.last(V))
                act(st_ef[:, 0:hw], st_a[:, 0:hw], ACT.Abs)
                tok_s = act(st_ef[:, hw:w_], st_b[:, 0:hw], ACT.Abs)
                p.wait(Tg, tok_s)
                nm = (w_ + 511) // 512
                for j in range(nm):
                    jw = min(512, w_ - j * 512)
                    p.wait(Tg, p.last(V))
                    tok_t = mm(ps[5][0:1, 0:jw], onec[:, :],
                               st_ef[:, j * 512:j * 512 + jw], True, True)
                    p.wait(V, tok_t)
                    cp(V, rowb[0:1, 0:jw], ps[5][0:1, 0:jw])
                    p.selfwait(V)
                    ts(V, rowb[0:1, 0:jw], rowb[0:1, 0:jw], 0.5 / 128.0,
                       QEPS * 0.5, ALU.mult, ALU.add)
                    repl_row(rowb[0:1, 0:jw],
                             xsc5[:, j * 512:j * 512 + jw], jw)
                p.selfwait(V)
                tok_v = ts(V, du[:, 0:w_], xsc5[:, 0:w_], -1.0, None,
                           ALU.mult)
                p.selfwait(V)
                p.wait(V, p.lastd("gpsimd"))
                tt(V, st_ef[:, 0:hw], st_a[:, 0:hw], xsc5[:, 0:hw], ALU.is_ge)
                tt(V, st_ef[:, hw:w_], st_b[:, 0:hw], xsc5[:, hw:w_],
                   ALU.is_ge)
                stt(V, st_wb[:, 0:hw], st_a[:, 0:hw], 1.0, du[:, 0:hw],
                    ALU.bypass, ALU.is_le)
                stt(V, st_wb[:, hw:w_], st_b[:, 0:hw], 1.0, du[:, hw:w_],
                    ALU.bypass, ALU.is_le)
                tt(V, st_wb[:, 0:w_], st_wb[:, 0:w_], st_ef[:, 0:w_],
                   ALU.subtract)
                p.selfwait(V)
                stt(V, st_wb[:, 0:w_], st_wb[:, 0:w_], 1.0, xsc5[:, 0:w_],
                    ALU.bypass, ALU.mult)
                tok_v = p.last(V)
                p.wait("gpsimd", tok_v)
                p.dma(dst_dram[rc_ * 128:(rc_ + 1) * 128, c0:c0 + w_],
                      st_wb[:, 0:w_], eng="gpsimd")

    if N_STAGES >= 2:
        w0_quant(wgT_in, DC, I0, tern_scr["g"], 0)
        w0_quant(wuT_in, DC, I0, tern_scr["u"], 1)
        w0_quant(wdT_in, 64, D, tern_scr["d"], 2)
    if N_STAGES >= 4:
        w1_quant(wg1T_in, DC, I1, tern_scr["g1"])
        w1_quant(wu1T_in, DC, I1, tern_scr["u1"])
        w1_quant(wd1T_in, 43, D, tern_scr["d1"])
        if DEBUG:
            p.wait("sync", p.lastd("gpsimd"))
            p.wait("sync", p.last(V))
            p.dma(dbg_tw[:, 0:5504], tern_scr["g1"][0:128, :])
            p.dma(dbg_tw[:, 5504:11008], tern_scr["u1"][0:128, :])
            p.dma(dbg_tw[:, 11008:13056], tern_scr["d1"][0:128, :])
            p.dma(dbg_tw[:, 13056:15104], tern_scr["d1"][128:256, :])
            p.wait(V, p.lastd("sync"))

    def ln_stats_res():
        for tc_ in range(TC):
            b0 = tc_ * D
            p.wait(Sg, p.last(V))
            act(xsc5[:, 0:1024], x_res[:, b0:b0 + 1024], ACT.Square,
                accum=ocol[:, 0:1])
            tok_s = act(xsc5[:, 1024:2048], x_res[:, b0 + 1024:b0 + 2048],
                        ACT.Square, accum=ocol[:, 1:2])
            red(V, scol[:, 0:1], x_res[:, b0:b0 + 1024], ALU.add)
            red(V, scol[:, 1:2], x_res[:, b0 + 1024:b0 + 2048], ALU.add)
            p.selfwait(V)
            tt(V, scol[:, 0:1], scol[:, 0:1], scol[:, 1:2], ALU.add)
            p.wait(V, tok_s)
            tt(V, ocol[:, 0:1], ocol[:, 0:1], ocol[:, 1:2], ALU.add)
            p.selfwait(V)
            ts(V, ongm[:, tc_:tc_ + 1], scol[:, 0:1], -1.0 / D, None, ALU.mult)
            p.selfwait(V)
            tt(V, scol[:, 2:3], ongm[:, tc_:tc_ + 1], ongm[:, tc_:tc_ + 1],
               ALU.mult)
            p.selfwait(V)
            ts(V, ocol[:, 0:1], ocol[:, 0:1], 1.0 / D, None, ALU.mult)
            p.selfwait(V)
            tok_v = tt(V, scol[:, 2:3], ocol[:, 0:1], scol[:, 2:3],
                       ALU.subtract)
            p.wait(Sg, tok_v)
            tok_s2 = act(scol[:, 3:4], scol[:, 2:3], ACT.Sqrt, bias=LNEPS)
            p.wait(V, tok_s2)
            vrecip(orst[:, tc_:tc_ + 1], scol[:, 3:4])
            p.selfwait(V)

    def build_xq(dst_fm, qmax, invs_dst):
        p.wait(V, p.lastd("gpsimd"))
        for tc_ in range(TC):
            b0 = tc_ * D
            p.wait(V, p.last(Tg))
            ts(V, xsc5[:, 0:2048], x_res[:, b0:b0 + 2048],
               ongm[:, tc_:tc_ + 1], orst[:, tc_:tc_ + 1], ALU.add, ALU.mult)
            p.selfwait(V)
            red(V, ocol[:, 0:1], xsc5[:, 0:2048], ALU.max, absv=True)
            p.selfwait(V)
            ts(V, ocol[:, 0:1], ocol[:, 0:1], QEPS, None, ALU.max)
            p.selfwait(V)
            tok_v = vrecip(scol[:, 0:1], ocol[:, 0:1])
            p.selfwait(V)
            ts(V, scol[:, 0:1], scol[:, 0:1], qmax, None, ALU.mult)
            ts(V, ocol2[:, 0:1], ocol[:, 0:1], 1.0 / qmax, None, ALU.mult)
            p.selfwait(V)
            ts(V, i8b[:, 0:2048], xsc5[:, 0:2048], scol[:, 0:1], None,
               ALU.mult)
            p.selfwait(V)
            tok_v = cp(V, st_wb[:, :], i8b[:, 0:2048])
            p.selfwait(V)
            tok_c = cp(V, ttile2[:, 0:1], ocol2[:, 0:1])
            p.wait(Tg, tok_c)
            tok_t = p.emit(Tg, lambda e: e.transpose(
                ps_bf2[0:1, 0:128], ttile2[:, 0:1], ident[:, :]))
            p.wait(V, tok_t)
            cp(V, ttile2[0:1, 0:128], ps_bf2[0:1, 0:128])
            p.selfwait(V)
            tok_c2 = cp(V, rowb[0:1, 0:128], ttile2[0:1, 0:128])
            p.wait(Tg, tok_c2)
            tok_t = mm(ps[5][:, 0:128], oner[:, :], rowb[0:1, 0:128],
                       True, True)
            p.wait(V, tok_t)
            cp(V, invs_dst[:, tc_ * 128:(tc_ + 1) * 128], ps[5][:, 0:128])
            p.wait(Tg, tok_v)
            vts = {}
            for dc_ in range(DC):
                pb = ps_bf if dc_ % 2 == 0 else ps_bf2
                if dc_ >= 2:
                    p.wait(Tg, vts[dc_ - 2])
                t_t = tr(pb[:, 0:128], st_wb[:, dc_ * 128:(dc_ + 1) * 128])
                p.wait(V, t_t)
                vts[dc_] = cp(V, dst_fm[:, dc_ * T + tc_ * 128:
                                        dc_ * T + tc_ * 128 + 128],
                              pb[:, 0:128])
            p.wait(Tg, vts[DC - 1]); p.wait(Tg, vts[DC - 2])

    if N_STAGES >= 2:
        ln_stats_res()
        build_xq(xq_fm, 128.0, invs)
        for tb in range(4):
            t0 = tb * 256
            for oc_ in range(64):
                sl_g = st_g[:, 0:2048] if oc_ % 2 == 0 else st_h[:, 0:2048]
                sl_u = (z1s[:, 0:2048] if oc_ % 2 == 0
                        else i8s[:, 0:4096].bitcast(BF16)[:, 0:2048])
                p.wait("sync", p.last(Tg))
                tg_ = p.dma(sl_g, tern_scr["g"][:, oc_ * 128:(oc_ + 1) * 128]
                            .rearrange("(a p) b -> p a b", p=128))
                tu_ = p.dma(sl_u, tern_scr["u"][:, oc_ * 128:(oc_ + 1) * 128]
                            .rearrange("(a p) b -> p a b", p=128))
                p.wait(Tg, tg_); p.wait(Tg, tu_)
                p.wait(Tg, p.last(V))
                for dc_ in range(DC):
                    mm(ps[0][:, 0:256], sl_g[:, dc_ * 128:(dc_ + 1) * 128],
                       xq_fm[:, dc_ * T + t0:dc_ * T + t0 + 256],
                       start=(dc_ == 0), stop=(dc_ == DC - 1))
                for dc_ in range(DC):
                    mm(ps[1][:, 0:256], sl_u[:, dc_ * 128:(dc_ + 1) * 128],
                       xq_fm[:, dc_ * T + t0:dc_ * T + t0 + 256],
                       start=(dc_ == 0), stop=(dc_ == DC - 1))
                tok_t = p.last(Tg)
                p.wait(V, tok_t)
                stt(V, sc_a[:, :], ps[0][:, 0:256], wsc[:, 0:1],
                    invs[:, t0:t0 + 256], ALU.mult, ALU.mult)
                tok_v = stt(V, sc_b[:, :], ps[1][:, 0:256], wsc[:, 1:2],
                            invs[:, t0:t0 + 256], ALU.mult, ALU.mult)
                p.wait(Sg, tok_v)
                tok_s = act(sc_c[:, :], sc_a[:, :], ACT.Silu)
                p.wait(V, tok_s)
                tt(V, inter[:, oc_ * 256:(oc_ + 1) * 256], sc_c[:, :],
                   sc_b[:, :], ALU.mult)
            if DEBUG and tb == 0:
                p.wait("sync", p.last(V))
                p.dma(dbg_i[0, :, :], inter[:, :])
                p.dma(dbg_t[:, :], tern_scr["g"][0:128, 6144:8192])
                p.dma(dbg_x2[:, :], xq_fm[:, :])
                p.dma(dbg_w[:, T:T + 16], wsc[:, :])
                p.dma(dbg_w[:, T + 16:T + 16 + TC], orst[:, :])
                p.wait(V, p.lastd("sync"))
            for sub in range(4):
                s0 = sub * 64
                p.wait(Tg, p.last(V))
                for lo in range(64):
                    if lo >= 1:
                        p.wait(Tg, p.last(V))
                    t_t = mm(ps[2][:, 0:64], h128[:, :],
                             inter[:, lo * 256 + s0:lo * 256 + s0 + 64],
                             True, True)
                    p.wait(V, t_t)
                    cp(V, z1s[:, 0:4096].rearrange("p (t l) -> p t l", l=64)[:, :, lo],
                       ps[2][:, 0:64])
                tok_v = p.last(V)
                p.wait(Tg, tok_v)
                for u_ in range(32):
                    pb = ps_bf if u_ % 2 == 0 else ps_bf2
                    tl = ttile if u_ % 2 == 0 else ttile2
                    if u_ >= 1:
                        p.wait(Tg, p.last(V))
                    t_t = tr(pb[:, 0:128], z1s[:, u_ * 128:(u_ + 1) * 128])
                    p.wait(V, t_t)
                    tok_c = cp(V, tl[:, :], pb[:, 0:128])
                    p.wait(Tg, tok_c)
                    mm(ps[3][:, 0:128], tl[:, :], i2h64[:, :], True, True)
                    tok_t2 = p.last(Tg)
                    p.wait(V, tok_t2)
                    cp(V, z2s[:, u_ * 128:(u_ + 1) * 128], ps[3][:, 0:128])
                p.selfwait(V)
                red(V, sc_a[:, 0:64],
                    z2s[:, 0:4096].rearrange("p (t l) -> p t l", l=64),
                    ALU.max, absv=True)
                tok_v = p.last(V)
                p.wait(Tg, tok_v)
                tok_t = p.emit(Tg, lambda e: e.transpose(
                    ps[4][0:64, 0:128], sc_a[:, 0:64].bitcast(F32),
                    identf[:, :]))
                p.wait(V, tok_t)
                red(V, sc_b[0:64, 0:1], ps[4][0:64, 0:128], ALU.max)
                p.selfwait(V)
                ts(V, sc_b[0:64, 0:1], sc_b[0:64, 0:1], QEPS / FNORM, None,
                   ALU.max)
                p.selfwait(V)
                vrecip(sc_c[0:64, 0:1], sc_b[0:64, 0:1])
                p.selfwait(V)
                ts(V, sc_c[0:64, 0:1], sc_c[0:64, 0:1], 128.0, None, ALU.mult)
                ts(V, sc_b[0:64, 0:1], sc_b[0:64, 0:1], FNORM / 128.0, None,
                   ALU.mult)
                p.selfwait(V)
                tok_c = cp(V, sc_d.bitcast(F32)[0:64, 0:1], sc_c[0:64, 0:1])
                p.wait(Tg, tok_c)
                tok_t = p.emit(Tg, lambda e: e.transpose(
                    ps[4][0:1, 0:64], sc_d.bitcast(F32)[0:64, 0:1],
                    identf[0:64, 0:64]))
                p.wait(V, tok_t)
                cp(V, rowb[0:1, 0:64], ps[4][0:1, 0:64])
                p.selfwait(V)
                repl_row(rowb[0:1, 0:64], sc_c[:, 0:64], 64, pb=4)
                tok_c = cp(V, sc_d.bitcast(F32)[0:64, 0:1], sc_b[0:64, 0:1])
                p.wait(Tg, tok_c)
                tok_t = p.emit(Tg, lambda e: e.transpose(
                    ps[4][0:1, 0:64], sc_d.bitcast(F32)[0:64, 0:1],
                    identf[0:64, 0:64]))
                p.wait(V, tok_t)
                cp(V, rowb[0:1, 0:64], ps[4][0:1, 0:64])
                p.selfwait(V)
                repl_row(rowb[0:1, 0:64], invs2[:, t0 + s0:t0 + s0 + 64], 64,
                         pb=4)
                p.selfwait(V)
                ts_in1 = sc_c[:, 0:64].unsqueeze(2).broadcast_to([128, 64, 64])
                p.emit(V, lambda e, b_=ts_in1: e.scalar_tensor_tensor(
                    out=i8s[:, 0:4096], in0=z2s[:, 0:4096], scalar=1.0,
                    in1=b_, op0=ALU.bypass, op1=ALU.mult))
                p.selfwait(V)
                p.emit(V, lambda e, s0_=s0: e.tensor_copy(
                    out=inter[:, s0_:s0_ + 16384:256]
                    .rearrange("p x -> p x") if False else
                    inter[:, 0:16384].rearrange("p (l t) -> p l t", t=256)
                    [:, :, s0_:s0_ + 64].transpose([0, 2, 1]),
                    in_=i8s[:, 0:4096].rearrange("p (t l) -> p t l", l=64)))
            if DEBUG and tb == 0:
                p.wait("sync", p.last(V))
                p.dma(dbg_i[1, :, :], inter[:, :])
                p.dma(dbg_z[:, :], z2s[:, :])
                p.dma(dbg_w[:, 0:T], invs2[:, :])
                p.wait(V, p.lastd("sync"))
                p.wait(Tg, p.lastd("sync"))
            for oc in range(DC):
                p.wait(Tg, p.last(V))
                for q_ in range(4):
                    slw = st_g[:, 0:2048] if q_ % 2 == 0 else st_h[:, 0:2048]
                    p.wait("sync", p.last(Tg))
                    tw = p.dma(slw, tern_scr["d"][q_ * 2048:(q_ + 1) * 2048,
                                                  oc * 128:(oc + 1) * 128]
                               .rearrange("(a p) b -> p a b", p=128))
                    p.wait(Tg, tw)
                    for kci in range(16):
                        kc = q_ * 16 + kci
                        mm(ps[0][:, 0:256], slw[:, kci * 128:(kci + 1) * 128],
                           inter[:, kc * 256:kc * 256 + 256],
                           start=(kc == 0), stop=(kc == 63))
                tok_t = p.last(Tg)
                p.wait(V, tok_t)
                stt(V, sc_e[:, :], ps[0][:, 0:256], wsc[:, 2:3],
                    invs2[:, t0:t0 + 256], ALU.mult, ALU.mult)
                p.selfwait(V)
                tok_v = cp(V, sc_d[:, :], sc_e[:, :])
                p.wait(Tg, tok_v)
                for i_ in range(2):
                    pb = ps_bf if i_ == 0 else ps_bf2
                    t_t = tr(pb[:, 0:128], sc_d[:, i_ * 128:(i_ + 1) * 128])
                    p.wait(V, t_t)
                    slr = x_res[:, (tb * 2 + i_) * D + oc * 128:
                                (tb * 2 + i_) * D + oc * 128 + 128]
                    tok_c = stt(V, slr, pb[:, 0:128], 1.0, slr, ALU.bypass,
                                ALU.add)
                p.wait(Tg, tok_c)



    def build_xhat_res(dst_fm):
        for tc_ in range(TC):
            b0 = tc_ * D
            p.wait(V, p.last(Tg))
            p.wait(V, p.lastd("gpsimd"))
            tok_v = ts(V, st_wb[:, 0:2048], x_res[:, b0:b0 + 2048],
                       ongm[:, tc_:tc_ + 1], orst[:, tc_:tc_ + 1],
                       ALU.add, ALU.mult)
            p.wait(Tg, tok_v)
            vts = {}
            for dc_ in range(DC):
                pb = ps_bf if dc_ % 2 == 0 else ps_bf2
                if dc_ >= 2:
                    p.wait(Tg, vts[dc_ - 2])
                t_t = tr(pb[:, 0:128], st_wb[:, dc_ * 128:(dc_ + 1) * 128])
                p.wait(V, t_t)
                vts[dc_] = cp(V, xln3[:, dc_ * T + tc_ * 128:
                                      dc_ * T + tc_ * 128 + 128], pb[:, 0:128])
            p.wait(Tg, vts[DC - 1]); p.wait(Tg, vts[DC - 2])

    def delta_proj(tb):
        t0 = tb * 256
        for dc_ in range(DC):
            p.wait(Tg, p.last(V))
            for rc_ in range(2):
                mm(ps[0][:, 0:256], wdtb[:, rc_ * 2048 + dc_ * 128:
                                         rc_ * 2048 + (dc_ + 1) * 128],
                   st_ef[:, rc_ * 1024 + t0:rc_ * 1024 + t0 + 256],
                   start=(rc_ == 0), stop=(rc_ == 1))
            tok_t = p.last(Tg)
            p.wait(Sg, tok_t); p.wait(V, tok_t)
            act(sc_a[:, :], ps[0][:, 0:256], ACT.Abs,
                bias=bdts[:, dc_:dc_ + 1])
            act(sc_b[:, :], sc_a[:, :], ACT.Exp, scale=-1.0)
            tok_s = act(sc_a[:, :], sc_b[:, :], ACT.Ln, bias=1.0)
            p.wait(V, tok_s)
            ts(V, sc_c[:, :], ps[0][:, 0:256], bdts[:, dc_:dc_ + 1], 0.0,
               ALU.add, ALU.max)
            p.selfwait(V)
            tt(V, delta[:, dc_ * 256:(dc_ + 1) * 256], sc_c[:, :], sc_a[:, :],
               ALU.add)
            tt(V, du[:, dc_ * 256:(dc_ + 1) * 256],
               delta[:, dc_ * 256:(dc_ + 1) * 256],
               xln3[:, dc_ * T + t0:dc_ * T + t0 + 256], ALU.mult)

    def load_wdt():
        for rc_ in range(2):
            p.wait("sync", p.last(V))
            t1 = p.dma(st_a[:, :], wdtT[rc_ * 128:(rc_ + 1) * 128, 0:1024])
            t2 = p.dma(st_b[:, :], wdtT[rc_ * 128:(rc_ + 1) * 128, 1024:2048])
            p.wait(V, t1); p.wait(V, t2)
            p.wait(V, p.last(Tg))
            cp(V, wdtb[:, rc_ * 2048:rc_ * 2048 + 1024], st_a[:, :])
            cp(V, wdtb[:, rc_ * 2048 + 1024:rc_ * 2048 + 2048], st_b[:, :])

    def build_bc_rep(tb, c_only=False):
        t0 = tb * 256
        p.wait(V, p.last(Tg))
        tok_v = cp(V, sc_d[0:16, :], u_sb[0:16, t0:t0 + 256])
        p.wait(Tg, tok_v)
        rng = range(8, 16) if c_only else range(16)
        for n_ in rng:
            dstt = c_rep if n_ >= 8 else b_rep
            t_t = mm(ps[1][:, 0:256], selc[:, n_ * 128:(n_ + 1) * 128],
                     sc_d[0:16, :], True, True)
            p.wait(V, t_t)
            tok_c = cp(V, dstt[:, (n_ % 8) * 256:(n_ % 8 + 1) * 256],
                       ps[1][:, 0:256])
            p.wait(Tg, tok_c)

    if N_STAGES >= 3:
        ln_stats_res()
        build_xhat_res(xln3)
        for c2 in range(2):
            c0 = 256 * 0 + c2 * 128
            p.wait("sync", p.last(V))
            wv3 = wxpT[:, c0:c0 + 128].rearrange("(a p) b -> p a b", p=128)
            t1 = p.dma(st_a[:, :], wv3[:, 0:8, :])
            t2 = p.dma(st_b[:, :], wv3[:, 8:16, :])
            p.wait(V, t1); p.wait(V, t2)
            p.wait(V, p.last(Tg))
            cp(V, st_wb[:, 0:1024], st_a[:, :])
            tok_v = cp(V, st_wb[:, 1024:2048], st_b[:, :])
            p.wait(Tg, tok_v)
            p.wait(Tg, p.last(V))
            for dc_ in range(DC):
                for t5 in range(2):
                    mm(ps[t5][:, :], st_wb[:, dc_ * 128:(dc_ + 1) * 128],
                       xln3[:, dc_ * T + t5 * 512:dc_ * T + t5 * 512 + 512],
                       start=(dc_ == 0), stop=(dc_ == DC - 1))
            tok_t = p.last(Tg)
            p.wait(V, tok_t)
            cp(V, st_ef[:, c2 * 1024:c2 * 1024 + 512], ps[0][:, :])
            cp(V, st_ef[:, c2 * 1024 + 512:c2 * 1024 + 1024], ps[1][:, :])
        p.wait("sync", p.last(V))
        bcv = wxpT[:, 256:272].rearrange("(a p) b -> p a b", p=128)
        t1 = p.dma(st_c[:, 0:256], bcv[:, :, :])
        p.wait(V, t1)
        tok_v = cp(V, sc_d[:, :], st_c[:, 0:256])
        p.wait(Tg, tok_v)
        p.wait(Tg, p.last(V))
        for dc_ in range(DC):
            for t5 in range(2):
                mm(ps[2][0:16, :], sc_d[:, dc_ * 16:(dc_ + 1) * 16],
                   xln3[:, dc_ * T + t5 * 512:dc_ * T + t5 * 512 + 512],
                   start=(dc_ == 0), stop=(dc_ == DC - 1)) if t5 == 0 else None
                mm(ps[3][0:16, :], sc_d[:, dc_ * 16:(dc_ + 1) * 16],
                   xln3[:, dc_ * T + t5 * 512:dc_ * T + t5 * 512 + 512],
                   start=(dc_ == 0), stop=(dc_ == DC - 1)) if t5 == 1 else None
        tok_t = p.last(Tg)
        p.wait(V, tok_t)
        cp(V, u_sb[0:16, 0:512], ps[2][0:16, :])
        cp(V, u_sb[0:16, 512:1024], ps[3][0:16, :])

        if DEBUG:
            tok_d = p.last(V)
            p.wait("sync", tok_d)
            p.dma(dbg_x2[:, :], xln3[:, :])
            p.dma(dbg_i[0, :, 0:2048], st_ef[:, :])
            p.dma(dbg_w[:, 0:T], u_sb[:, 0:1024])
            p.wait(V, p.lastd("sync"))
        for tb in range(4 if P6_SUB >= 2 else 0):
            t0 = tb * 256
            load_wdt()
            if P6_SUB < 3:
                continue
            delta_proj(tb)
            if P6_SUB < 4:
                continue
            build_bc_rep(tb)
            if P6_SUB < 5:
                continue
            p.wait(Sg, p.last(V))
            for dc_ in range(DC):
                for n_ in range(NST):
                    tok_s = act(sc_a[:, :], delta[:, dc_ * 256:(dc_ + 1) * 256],
                                ACT.Exp,
                                scale=aneg[:, dc_ * NST + n_:dc_ * NST + n_ + 1])
                    stt(V, sc_b[:, :], du[:, dc_ * 256:(dc_ + 1) * 256], 1.0,
                        b_rep[:, n_ * 256:(n_ + 1) * 256], ALU.bypass, ALU.mult)
                    p.selfwait(V)
                    p.wait(V, tok_s)
                    tok_v = p.emit(V, lambda e, n_=n_, dc_=dc_: e.tensor_tensor_scan(
                        out=hbuf[:, n_ * 256:(n_ + 1) * 256], data0=sc_a[:, :],
                        data1=sc_b[:, :],
                        initial=hstate[:, dc_ * NST + n_:dc_ * NST + n_ + 1],
                        op0=ALU.mult, op1=ALU.add))
                    p.wait(Sg, tok_v)
                if DEBUG and tb == 0 and dc_ == 0:
                    p.selfwait(V)
                    p.wait(G, p.last(V))
                    p.dma(dbg_z[:, 0:2048], hbuf[:, 0:2048], eng="gpsimd")
                    p.dma(dbg_i[1, :, 0:4096], delta[:, :], eng="gpsimd")
                    p.wait(V, p.lastd("gpsimd"))
                p.selfwait(V)
                cp(V, hstate[:, dc_ * NST:(dc_ + 1) * NST],
                   hbuf[:, 0:2048].rearrange("p (n t) -> p n t", t=256)[:, :, 255])
                p.selfwait(V)
                tt(V, hbuf[:, 0:2048], hbuf[:, 0:2048], c_rep[:, 0:2048],
                   ALU.mult)
                p.selfwait(V)
                red(V, sc_c[:, :],
                    hbuf[:, 0:2048].rearrange("p (n t) -> p t n", t=256),
                    ALU.add)
                p.selfwait(V)
                stt(V, sc_e[:, :], xln3[:, dc_ * T + t0:dc_ * T + t0 + 256],
                    dpar[:, dc_:dc_ + 1], sc_c[:, :], ALU.mult, ALU.add)
                p.selfwait(V)
                tok_v = cp(V, sc_d[:, :], sc_e[:, :])
                p.wait(Tg, tok_v)
                for i_ in range(2):
                    pb = ps_bf if i_ == 0 else ps_bf2
                    t_t = tr(pb[:, 0:128], sc_d[:, i_ * 128:(i_ + 1) * 128])
                    p.wait(V, t_t)
                    slr = x_res[:, (tb * 2 + i_) * D + dc_ * 128:
                                (tb * 2 + i_) * D + dc_ * 128 + 128]
                    tok_c = stt(V, slr, pb[:, 0:128], 1.0, slr, ALU.bypass,
                                ALU.add)
                p.wait(Tg, tok_c)
        pass
    if N_STAGES >= 3 and P6_SUB >= 6:
        p.selfwait(V)
        ts(V, fsend[:, 128:256], hstate[:, :], flag[:, 0:1], None, ALU.mult)
        p.selfwait(V)
        tok_v = tt(V, fsend[:, 0:128], hstate[:, :], fsend[:, 128:256],
                   ALU.subtract)
        p.wait(G, tok_v)
        tcc = p.dma(cc_in, fsend[:, :], eng="gpsimd")
        p.wait(G, tcc)
        tok_g = p.emit(G, lambda e: e.collective_compute(
            "AllReduce", ALU.add,
            replica_groups=[[0, 1], [2, 3], [4, 5], [6, 7]],
            ins=[cc_in.opt()], outs=[cc_out.opt()]))
        p.wait(G, tok_g)
        trc = p.dma(frecv[:, :], cc_out, eng="gpsimd")
        p.wait(V, trc)
        tok_v = ts(V, fw[:, :], frecv[:, 0:128], flag[:, 0:1], None, ALU.mult)
        for tb in range(4 if P6_SUB >= 7 else 0):
            t0 = tb * 256
            load_wdt()
            delta_proj(tb)
            build_bc_rep(tb, c_only=True)
            for dc_ in range(DC):
                p.selfwait(V)
                tok_v = p.emit(V, lambda e, dc_=dc_: e.tensor_tensor_scan(
                    out=sc_e[:, :], data0=delta[:, dc_ * 256:(dc_ + 1) * 256],
                    data1=zero64[:, :],
                    initial=cdcar[:, dc_:dc_ + 1], op0=ALU.add, op1=ALU.add))
                p.selfwait(V)
                cp(V, cdcar[:, dc_:dc_ + 1], sc_e[:, 255:256])
                p.wait(Sg, p.last(V))
                for n_ in range(NST):
                    tok_s = act(sc_a[:, :], sc_e[:, :], ACT.Exp,
                                scale=aneg[:, dc_ * NST + n_:dc_ * NST + n_ + 1])
                    p.wait(V, tok_s)
                    if n_ == 0:
                        stt(V, sc_c[:, :], sc_a[:, :],
                            fw[:, dc_ * NST:dc_ * NST + 1],
                            c_rep[:, 0:256], ALU.mult, ALU.mult)
                    else:
                        stt(V, sc_b[:, :], sc_a[:, :],
                            fw[:, dc_ * NST + n_:dc_ * NST + n_ + 1],
                            c_rep[:, n_ * 256:(n_ + 1) * 256], ALU.mult,
                            ALU.mult)
                        p.selfwait(V)
                        tt(V, sc_c[:, :], sc_c[:, :], sc_b[:, :], ALU.add)
                    tok_v2 = p.last(V)
                    p.wait(Sg, tok_v2)
                p.selfwait(V)
                tok_v = cp(V, sc_d[:, :], sc_c[:, :])
                p.wait(Tg, tok_v)
                for i_ in range(2):
                    pb = ps_bf if i_ == 0 else ps_bf2
                    t_t = tr(pb[:, 0:128], sc_d[:, i_ * 128:(i_ + 1) * 128])
                    p.wait(V, t_t)
                    slr = x_res[:, (tb * 2 + i_) * D + dc_ * 128:
                                (tb * 2 + i_) * D + dc_ * 128 + 128]
                    tok_c = stt(V, slr, pb[:, 0:128], 1.0, slr, ALU.bypass,
                                ALU.add)
                p.wait(Tg, tok_c)

    if N_STAGES >= 4:
        ln_stats_res()
        build_xq(xq1_fm, 127.0, invs)
        if DEBUG:
            p.wait("sync", p.last(V))
            p.wait("sync", p.last(Tg))
            p.dma(dbg_x2[:, :], xq1_fm[:, :])
            p.dma(dbg_dn[:, 0:T], invs[:, :])
            p.wait(V, p.lastd("sync"))
            p.wait(Tg, p.lastd("sync"))
        for tb in range(4):
            t0 = tb * 256
            for oc1 in range(43):
                sl_g = st_g[:, 0:2048] if oc1 % 2 == 0 else st_h[:, 0:2048]
                sl_u = (z1s[:, 0:2048] if oc1 % 2 == 0
                        else i8s[:, 0:4096].bitcast(BF16)[:, 0:2048])
                p.wait("sync", p.last(Tg))
                tg_ = p.dma(sl_g, tern_scr["g1"][:, oc1 * 128:(oc1 + 1) * 128]
                            .rearrange("(a p) b -> p a b", p=128))
                tu_ = p.dma(sl_u, tern_scr["u1"][:, oc1 * 128:(oc1 + 1) * 128]
                            .rearrange("(a p) b -> p a b", p=128))
                p.wait(Tg, tg_); p.wait(Tg, tu_)
                p.wait(Tg, p.last(V))
                for dc_ in range(DC):
                    mm(ps[0][:, 0:256], sl_g[:, dc_ * 128:(dc_ + 1) * 128],
                       xq1_fm[:, dc_ * T + t0:dc_ * T + t0 + 256],
                       start=(dc_ == 0), stop=(dc_ == DC - 1))
                for dc_ in range(DC):
                    mm(ps[1][:, 0:256], sl_u[:, dc_ * 128:(dc_ + 1) * 128],
                       xq1_fm[:, dc_ * T + t0:dc_ * T + t0 + 256],
                       start=(dc_ == 0), stop=(dc_ == DC - 1))
                tok_t = p.last(Tg)
                p.wait(V, tok_t)
                stt(V, sc_a[:, :], ps[0][:, 0:256], -2.0,
                    invs[:, t0:t0 + 256], ALU.mult, ALU.mult)
                tok_v = stt(V, sc_b[:, :], ps[1][:, 0:256], -2.0,
                            invs[:, t0:t0 + 256], ALU.mult, ALU.mult)
                p.wait(Sg, tok_v)
                tok_s = act(sc_c[:, :], sc_a[:, :], ACT.Silu)
                p.wait(V, tok_s)
                tt(V, hid[:, oc1 * 256:(oc1 + 1) * 256], sc_c[:, :],
                   sc_b[:, :], ALU.mult)
                p.selfwait(V)
                if oc1 == 0:
                    ts(V, sc_e[:, :], hid[:, 0:256], -1.0, None, ALU.mult)
                    p.selfwait(V)
                    tt(V, sc_e[:, :], sc_e[:, :], hid[:, 0:256], ALU.max)
                else:
                    stt(V, sc_e[:, :], hid[:, oc1 * 256:(oc1 + 1) * 256],
                        -1.0, sc_e[:, :], ALU.mult, ALU.max)
                    p.selfwait(V)
                    tt(V, sc_e[:, :], sc_e[:, :],
                       hid[:, oc1 * 256:(oc1 + 1) * 256], ALU.max)
            tok_v = p.last(V)
            p.wait(Tg, tok_v)
            for i_ in range(2):
                tok_t = p.emit(Tg, lambda e, i_=i_: e.transpose(
                    ps[2][:, 0:128], sc_e[:, i_ * 128:(i_ + 1) * 128],
                    identf[:, :]))
                p.wait(V, tok_t)
                red(V, scol[:, 0:1], ps[2][:, 0:128], ALU.max)
                p.selfwait(V)
                ts(V, scol[:, 0:1], scol[:, 0:1], QEPS, None, ALU.max)
                p.selfwait(V)
                vrecip(scol[:, 1:2], scol[:, 0:1])
                p.selfwait(V)
                ts(V, scol[:, 1:2], scol[:, 1:2], 127.0, None, ALU.mult)
                ts(V, scol[:, 2:3], scol[:, 0:1], 1.0 / 127.0, None, ALU.mult)
                p.selfwait(V)
                tok_c = cp(V, ttile2[:, 0:1], scol[:, 1:2].bitcast(F32))
                p.wait(Tg, tok_c)
                tok_t = p.emit(Tg, lambda e: e.transpose(
                    ps_bf2[0:1, 0:128], ttile2[:, 0:1], ident[:, :]))
                p.wait(V, tok_t)
                cp(V, ttile2[0:1, 0:128], ps_bf2[0:1, 0:128])
                p.selfwait(V)
                tok_c = cp(V, rowb[0:1, 0:128], ttile2[0:1, 0:128])
                p.wait(Tg, tok_c)
                tok_t = mm(ps[3][:, 0:128], oner[:, :], rowb[0:1, 0:128],
                           True, True)
                p.wait(V, tok_t)
                cp(V, sc_a[:, i_ * 128:(i_ + 1) * 128], ps[3][:, 0:128])
                p.selfwait(V)
                tok_c = cp(V, ttile2[:, 0:1], scol[:, 2:3].bitcast(F32))
                p.wait(Tg, tok_c)
                tok_t = p.emit(Tg, lambda e: e.transpose(
                    ps_bf2[0:1, 0:128], ttile2[:, 0:1], ident[:, :]))
                p.wait(V, tok_t)
                cp(V, ttile2[0:1, 0:128], ps_bf2[0:1, 0:128])
                p.selfwait(V)
                tok_c = cp(V, rowb[0:1, 0:128], ttile2[0:1, 0:128])
                p.wait(Tg, tok_c)
                tok_t = mm(ps[3][:, 0:128], oner[:, :], rowb[0:1, 0:128],
                           True, True)
                p.wait(V, tok_t)
                cp(V, invs2[:, t0 + i_ * 128:t0 + (i_ + 1) * 128],
                   ps[3][:, 0:128])
            if DEBUG and tb == 0:
                p.selfwait(V)
                p.wait("sync", p.last(V))
                p.dma(dbg_i[0, :, 0:11008], hid[:, :])
                p.dma(dbg_w[:, 0:256], sc_a[:, :])
                p.dma(dbg_w[:, 256:512], invs2[:, 0:256])
                p.wait(V, p.lastd("sync"))
            for oc1 in range(43):
                p.selfwait(V)
                stt(V, sc_i8[:, :], hid[:, oc1 * 256:(oc1 + 1) * 256], 1.0,
                    sc_a[:, 0:256], ALU.bypass, ALU.mult)
                p.selfwait(V)
                cp(V, hid[:, oc1 * 256:(oc1 + 1) * 256], sc_i8[:, :])
            if DEBUG and tb == 0:
                p.selfwait(V)
                p.wait("sync", p.last(V))
                p.dma(dbg_i[1, :, 0:11008], hid[:, :])
                p.wait(V, p.lastd("sync"))
            for oc in range(DC):
                p.wait(Tg, p.last(V))
                nks = [16, 16, 11]
                for q_ in range(3):
                    slw = st_g[:, 0:2048] if q_ % 2 == 0 else st_h[:, 0:2048]
                    nk = nks[q_]
                    p.wait("sync", p.last(Tg))
                    tw = p.dma(slw[:, 0:nk * 128],
                               tern_scr["d1"][q_ * 2048:q_ * 2048 + nk * 128,
                                              oc * 128:(oc + 1) * 128]
                               .rearrange("(a p) b -> p a b", p=128))
                    p.wait(Tg, tw)
                    for kci in range(nk):
                        kc = q_ * 16 + kci
                        mm(ps[0][:, 0:256], slw[:, kci * 128:(kci + 1) * 128],
                           hid[:, kc * 256:kc * 256 + 256],
                           start=(kc == 0), stop=(kc == 42))
                tok_t = p.last(Tg)
                p.wait(V, tok_t)
                stt(V, sc_e[:, :], ps[0][:, 0:256], -2.0,
                    invs2[:, t0:t0 + 256], ALU.mult, ALU.mult)
                p.selfwait(V)
                tok_v = cp(V, sc_d[:, :], sc_e[:, :])
                p.wait(Tg, tok_v)
                for i_ in range(2):
                    pb = ps_bf if i_ == 0 else ps_bf2
                    t_t = tr(pb[:, 0:128], sc_d[:, i_ * 128:(i_ + 1) * 128])
                    p.wait(V, t_t)
                    slr = x_res[:, (tb * 2 + i_) * D + oc * 128:
                                (tb * 2 + i_) * D + oc * 128 + 128]
                    tok_c = stt(V, slr, pb[:, 0:128], 1.0, slr, ALU.bypass,
                                ALU.add)
                p.wait(Tg, tok_c)

    tok_fin = p.last(V)
    p.wait("sync", tok_fin)
    for tc_ in range(TC):
        p.dma(out[tc_ * 128:(tc_ + 1) * 128, :], x_res[:, tc_ * D:(tc_ + 1) * D])
    tok_st = p.lastd("sync")
    p.wait("sync", tok_st)

    p.run()
    return nc


def _hada(n):
    h = np.array([[1.0]], dtype=np.float32)
    while h.shape[0] < n:
        h = np.block([[h, h], [h, -h]])
    return h


def kernel(L0_attn_hidden_states, L0_attn_Wq, L0_attn_bq, L0_attn_Wk,
           L0_attn_bk, L0_attn_Wv, L0_attn_bv, L0_attn_Wo, L0_attn_bo,
           L0_attn_alibi_slopes, L0_ffn_w_gate, L0_ffn_w_up, L0_ffn_w_down,
           L1_attn_W_xproj, L1_attn_W_dt, L1_attn_b_dt, L1_attn_A_log,
           L1_attn_D_param, L1_ffn_w_gate, L1_ffn_w_up, L1_ffn_w_down):
    xs = np.asarray(L0_attn_hidden_states, dtype=np.float32)

    def tp(a):
        return np.ascontiguousarray(np.asarray(a, dtype=np.float32).T)

    def bias_cols(b):
        return np.ascontiguousarray(
            np.asarray(b, dtype=np.float32).reshape(DC, 128).T)

    H64 = _hada(64)
    perm0 = np.array([(pp * 64 + cc) for cc in range(64) for pp in range(128)])
    shared = {
        "wqT": tp(L0_attn_Wq), "wkT": tp(L0_attn_Wk),
        "wvT": tp(L0_attn_Wv), "woT": tp(L0_attn_Wo),
        "bq": bias_cols(L0_attn_bq), "bk": bias_cols(L0_attn_bk),
        "bv": bias_cols(L0_attn_bv), "bo": bias_cols(L0_attn_bo),
        "slo": np.asarray(L0_attn_alibi_slopes, np.float32).reshape(1, 16),
        "wgT": np.ascontiguousarray(tp(L0_ffn_w_gate)[:, perm0]),
        "wuT": np.ascontiguousarray(tp(L0_ffn_w_up)[:, perm0]),
        "wdT": np.ascontiguousarray(tp(L0_ffn_w_down)[perm0, :]),
        "wxpT": tp(L1_attn_W_xproj), "wdtT": tp(L1_attn_W_dt),
        "bdt": bias_cols(L1_attn_b_dt),
        "alog": np.ascontiguousarray(
            np.asarray(L1_attn_A_log, np.float32).reshape(DC, 128, NST)
            .transpose(1, 0, 2).reshape(128, DC * NST)),
        "dpar": bias_cols(L1_attn_D_param),
        "wg1T": tp(L1_ffn_w_gate), "wu1T": tp(L1_ffn_w_up),
        "wd1T": tp(L1_ffn_w_down),
        "h128c": _hada(128).astype(ml_dtypes.bfloat16),
        "i2h64c": np.kron(np.eye(2), H64).astype(ml_dtypes.bfloat16),
        "identc": np.eye(128, dtype=ml_dtypes.bfloat16),
        "onecc": np.ones((128, 1), dtype=ml_dtypes.bfloat16),
        "onerc": np.ones((1, 128), dtype=np.float32),
        "identfc": np.eye(128, dtype=np.float32),
        "selc": np.kron(np.eye(16), np.ones((1, 128))).astype(ml_dtypes.bfloat16),
    }

    in_maps = []
    for c in range(8):
        b, h = c // 2, c % 2
        tq0 = h * T
        x_own = np.ascontiguousarray(xs[b, tq0:tq0 + T, :])
        pcol = np.arange(128)[:, None]
        w = np.arange(2944)[None, :]
        rel = pcol - w + 1920 - tq0
        master = np.where(rel <= 0, rel.astype(np.float32),
                          np.float32(-1e30)).astype(np.float32)
        m = dict(shared)
        m.update({
            "x_own": x_own,
            "x_seq": np.ascontiguousarray(xs[b]),
            "master": master,
            "flag": np.full((128, 1), float(h), np.float32),
        })
        in_maps.append(m)

    nc = build_nc()
    res = run_bass_kernel_spmd(nc, in_maps, core_ids=list(range(8)))
    global LAST_RESULTS
    LAST_RESULTS = [{k: np.asarray(v) for k, v in r.items()}
                    for r in res.results]
    out = np.empty((B, S, D), dtype=np.float32)
    for c in range(8):
        b, h = c // 2, c % 2
        out[b, h * T:(h + 1) * T, :] = res.results[c]["out"]
    return out
